# version 05
# baseline (speedup 1.0000x reference)
import contextlib
import numpy as np
import concourse.bass as bass
import concourse.mybir as mybir
from concourse.bass_utils import run_bass_kernel_spmd

F32 = mybir.dt.float32
BF16 = mybir.dt.bfloat16
U8 = mybir.dt.uint8
AF = mybir.ActivationFunctionType
OP = mybir.AluOpType

NCORES = 8
D = 2048
NS = 1024
NM = 16384
TPC = 2048
TB = 512
NBLK = TPC // TB
FD = 1024
RS = 3
NGT = NM // FD
NUNIT = NM // TB
ECH = NM // 128
GSZ = 8
LN05 = float(np.log(0.5))

C_M12, C_M4, C_B4, C_B1275, C_EPS, C_LN05, C_ZERO, C_SP = range(8)

DEBUG_DUMPS = False


def _perm(n):
    blk = np.concatenate([np.arange(0, 256, 2), np.arange(1, 256, 2)])
    return (np.arange(n // 256)[:, None] * 256 + blk[None, :]).ravel()


def build_program():
    nc = bass.Bass("TRN2", target_bir_lowering=False)

    xTs = nc.dram_tensor("xTs", [D, TPC], F32, kind="ExternalInput").ap()
    gpT = nc.dram_tensor("gpT", [1024, NM], U8, kind="ExternalInput").ap()
    upT = nc.dram_tensor("upT", [1024, NM], U8, kind="ExternalInput").ap()
    dpT = nc.dram_tensor("dpT", [8192, D], U8, kind="ExternalInput").ap()
    gsT = nc.dram_tensor("gsT", [64, NM], F32, kind="ExternalInput").ap()
    usT = nc.dram_tensor("usT", [64, NM], F32, kind="ExternalInput").ap()
    dsT = nc.dram_tensor("dsT", [512, D], F32, kind="ExternalInput").ap()
    BTp = nc.dram_tensor("BTp", [D, NS], F32, kind="ExternalInput").ap()
    CTp = nc.dram_tensor("CTp", [NS, D], F32, kind="ExternalInput").ap()
    Dp = nc.dram_tensor("Dp", [128, 16], F32, kind="ExternalInput").ap()
    cstI = nc.dram_tensor("cstI", [128, 8], F32, kind="ExternalInput").ap()
    onesI = nc.dram_tensor("onesI", [128, 1], F32, kind="ExternalInput").ap()
    oT = nc.dram_tensor("oT", [D, TPC], F32, kind="ExternalOutput").ap()
    dbg = {}
    if DEBUG_DUMPS:
        import os
        if os.environ.get("DBG_FULL_SCRATCH"):
            for nm, shp, dt2 in (("dbg_wgS", [16, 128, NM], BF16),
                                 ("dbg_wuS", [16, 128, NM], BF16),
                                 ("dbg_wdS", [128, ECH, D], BF16)):
                dbg[nm] = nc.dram_tensor(nm, shp, dt2, kind="ExternalOutput").ap()
        for nm, shp in (("dbg_wg", [128, 512]), ("dbg_wd", [128, 512]),
                        ("dbg_acc0", [128, 512]), ("dbg_h1", [128, 512]),
                        ("dbg_mean", [1, 512]), ("dbg_rstd", [1, 512]),
                        ("dbg_stb", [128, 512]), ("dbg_acc1", [128, 512]),
                        ("dbg_meanb", [128, 512]), ("dbg_rstdb", [128, 512]),
                        ("dbg_thalf", [128, 512]),
                        ("dbg_h2", [128, 512]), ("dbg_inter", [128, 512])):
            dbg[nm] = nc.dram_tensor(nm, shp, F32, kind="ExternalOutput").ap()

    wgS = nc.dram_tensor("wgS", [16, 128, NM], BF16, kind="Internal").ap()
    wuS = nc.dram_tensor("wuS", [16, 128, NM], BF16, kind="Internal").ap()
    wdS = nc.dram_tensor("wdS", [128, ECH, D], BF16, kind="Internal").ap()
    statH = nc.dram_tensor("statH", [2, TB], F32, kind="Internal").ap()

    ctx = contextlib.ExitStack()
    with ctx as st_:
        def sbt(name, shape, dt):
            return st_.enter_context(nc.sbuf_tensor(name, shape, dt))

        arU8 = sbt("arU8", [128, 5 * RS * FD], U8)
        arBF = sbt("arBF", [128, 49152], BF16)
        h_sb = sbt("h_sb", [128, 8192], BF16)
        inter_sb = sbt("inter_sb", [128, 4096], BF16)
        g1_sb = sbt("g1_sb", [128, 2 * TB], BF16)
        arF32 = sbt("arF32", [128, 13312], F32)
        u8pad = sbt("u8pad", [1, 64], U8)
        f32pad = sbt("f32pad", [1, 64], F32)
        cst = sbt("cst", [128, 8], F32)
        dv = sbt("dv", [128, 16], F32)
        ones_f = sbt("ones_f", [128, 1], F32)
        ones_b = sbt("ones_b", [128, 1], BF16)
        s_mean = sbt("s_mean", [1, TB], F32)
        s_ex2 = sbt("s_ex2", [1, TB], F32)
        s_mm2 = sbt("s_mm2", [1, TB], F32)
        s_var = sbt("s_var", [1, TB], F32)
        s_lnv = sbt("s_lnv", [1, TB], F32)
        s_rstd = sbt("s_rstd", [1, TB], F32)

        ps_g = [st_.enter_context(nc.psum_tensor(f"ps_g{i}", [128, TB], F32)) for i in range(2)]
        ps_u = [st_.enter_context(nc.psum_tensor(f"ps_u{i}", [128, TB], F32)) for i in range(2)]
        ps_d = [st_.enter_context(nc.psum_tensor(f"ps_d{i}", [128, TB], F32)) for i in range(2)]
        ps_s1 = st_.enter_context(nc.psum_tensor("ps_s1", [1, TB], F32))
        ps_s2 = st_.enter_context(nc.psum_tensor("ps_s2", [1, TB], F32))

        semnames = ["s_ld", "s_pk", "s_scl", "s_dva", "s_dvb", "s_dvc",
                    "s_aca", "s_acb", "s_poo", "s_wr", "s_x", "s_bc",
                    "s_stat", "s_brd", "s_h", "s_pg", "s_puu", "s_pd",
                    "s_pa", "s_ge", "s_it", "s_ev", "s_ul", "s_st", "s_bar",
                    "s_pre"]
        SEM = {n: st_.enter_context(nc.semaphore(n)) for n in semnames}

        U8_PK, U8_T8, U8_HI8, U8_M8L, U8_M8H = (i * RS * FD for i in range(5))

        def u8v(base, slot):
            o = base + (slot % RS) * FD
            return arU8[:, o:o + FD]

        (K_MBL, K_MBH, K_SGL, K_SGH, K_R6L, K_R6H,
         K_QL, K_QH, K_Q2L, K_Q2H, K_VL, K_VH) = range(12)

        def bfv(kind, slot):
            o = (kind * RS + (slot % RS)) * FD
            return arBF[:, o:o + FD]

        bt_t = arBF[:, 0:16384].rearrange("p (k n) -> p k n", k=16)
        ct_t = arBF[:, 16384:32768].rearrange("p (k n) -> p k n", k=8)
        xbb_t = arBF[:, 32768:40960].rearrange("p (k t) -> p k t", k=16)
        sq_t = arBF[:, 40960:49152].rearrange("p (k t) -> p k t", k=16)
        stb_t = arBF[:, 32768:36864].rearrange("p (k t) -> p k t", k=8)

        def wg_u(slot):
            return arBF[:, slot * 8192:(slot + 1) * 8192].rearrange("p (k n) -> p k n", k=16)

        def wu_u(slot):
            o = 16384 + slot * 8192
            return arBF[:, o:o + 8192].rearrange("p (k n) -> p k n", k=16)

        def wd_u(slot):
            o = 32768 + slot * 8192
            return arBF[:, o:o + 8192].rearrange("p (e h) -> p e h", e=4)

        h_t = h_sb[:].rearrange("p (k t) -> p k t", k=16)
        inter_t = inter_sb[:].rearrange("p (e t) -> p e t", e=8)

        def g1_t(slot):
            return g1_sb[:, slot * TB:(slot + 1) * TB]

        acc_t = arF32[:, 0:8192].rearrange("p (k t) -> p k t", k=16)

        def sclv(slot):
            o = 8192 + (slot % RS) * FD
            return arF32[:, o:o + FD]

        thalf_t = arF32[:, 8192:12288].rearrange("p (k t) -> p k t", k=8)
        mean_b = arF32[:, 12288:12800]
        rstd_b = arF32[:, 12800:13312]

        progs = {e: [] for e in ("sync", "gpsimd", "vector", "scalar", "tensor")}
        cnt = {n: 0 for n in semnames}
        nbar = [0]

        def emit(eng, fn):
            progs[eng].append(fn)

        def W(eng, sem, val):
            if val > 0:
                emit(eng, lambda e, s=SEM[sem], v=val: e.wait_ge(s, v))

        def INC(sem, k=1):
            cnt[sem] += k
            return cnt[sem]

        def DMA(eng, out, in_, sem):
            v = INC(sem, 16)
            emit(eng, lambda e, o=out, i=in_, s=SEM[sem]: e.dma_start(out=o, in_=i).then_inc(s, 16))
            return v

        def barrier():
            nbar[0] += 1
            tgt = 5 * nbar[0]
            for eng in progs:
                emit(eng, lambda e, s=SEM["s_bar"]: e.nop().then_inc(s, 1))
                emit(eng, lambda e, s=SEM["s_bar"], v=tgt: e.wait_ge(s, v))


        def DBG(name, src_ap, eng="sync", conv=False):
            if not DEBUG_DUMPS or name not in dbg:
                return
            dbg_f = arF32[:, 8192:8704]
            if conv:
                v = INC("s_ld", 16)
                emit(eng, lambda e, o=dbg_f, i=src_ap, s=SEM["s_ld"]: e.dma_start(out=o, in_=i).then_inc(s, 16))
                W(eng, "s_ld", v)
                emit(eng, lambda e, o=dbg[name], i=dbg_f, s=SEM["s_st"]: e.dma_start(out=o, in_=i).then_inc(s, 16))
            else:
                emit(eng, lambda e, o=dbg[name], i=src_ap, s=SEM["s_st"]: e.dma_start(out=o, in_=i).then_inc(s, 16))
            INC("s_st", 16)

        DMA("sync", cst[:], cstI, "s_ld")
        DMA("sync", dv[:], Dp, "s_ld")
        DMA("sync", ones_f[:], onesI, "s_ld")
        W("vector", "s_ld", 48)
        emit("vector", lambda e: e.tensor_copy(ones_b[:], ones_f[:]).then_inc(SEM["s_pre"], 1))
        INC("s_pre")

        tiles = []
        for psrc, ssrc, dst in ((gpT, gsT, wgS), (upT, usT, wuS)):
            for nr in range(NGT):
                for c in range(8):
                    tiles.append(("gu", psrc, ssrc, dst, c, nr))
        for cd in range(64):
            for hr in range(2):
                tiles.append(("d", dpT, dsT, wdS, cd, hr))
        NT = len(tiles)

        def tile_srcs(i):
            kind, psrc, ssrc, dst, c, nr = tiles[i]
            pk_ap = psrc[128 * c:128 * (c + 1), FD * nr:FD * (nr + 1)]
            scl_ap = (ssrc[8 * c:8 * c + 8, FD * nr:FD * (nr + 1)]
                      .unsqueeze(1).broadcast_to([8, 16, FD]))
            return pk_ap, scl_ap

        def tile_dsts(i):
            kind, psrc, ssrc, dst, c, nr = tiles[i]
            if kind == "gu":
                return (dst[2 * c, :, FD * nr:FD * (nr + 1)],
                        dst[2 * c + 1, :, FD * nr:FD * (nr + 1)])
            return (dst[:, 2 * c, FD * nr:FD * (nr + 1)],
                    dst[:, 2 * c + 1, FD * nr:FD * (nr + 1)])

        for i in range(NT):
            pk_ap, _ = tile_srcs(i)
            if i >= RS:
                W("sync", "s_aca", 5 * (i - RS + 1))
                W("sync", "s_dva", 3 * (i - RS + 1))
            DMA("sync", u8v(U8_PK, i)[:], pk_ap, "s_pk")
        DMA("sync", u8pad[:], gpT[0:1, 0:64], "s_pk")
        assert cnt["s_pk"] == 16 * (NT + 1)

        def emit_vecA(i):
            sl = i
            pk = u8v(U8_PK, sl)
            W("vector", "s_pk", 16 * (i + 2))
            emit("vector", lambda e, a=u8v(U8_T8, sl), b=pk: e.tensor_scalar(a[:], b[:], 8, None, OP.bitwise_and).then_inc(SEM["s_dva"], 1))
            emit("vector", lambda e, a=u8v(U8_HI8, sl), b=pk: e.tensor_scalar(a[:], b[:], 4, None, OP.logical_shift_right))
            emit("vector", lambda e, a=u8v(U8_M8L, sl), b=pk: e.tensor_scalar(a[:], b[:], 7, None, OP.bitwise_and).then_inc(SEM["s_dva"], 1))
            emit("vector", lambda e, a=u8v(U8_M8H, sl), b=u8v(U8_HI8, sl): e.tensor_scalar(a[:], b[:], 7, None, OP.bitwise_and).then_inc(SEM["s_dva"], 1))
            emit("vector", lambda e, a=bfv(K_SGL, sl), b=u8v(U8_T8, sl): e.tensor_scalar(a[:], b[:], -0.25, 1.0, OP.mult, OP.add))

        for k in range(min(RS, NT)):
            emit_vecA(k)
        for i in range(NT):
            sl = i
            W("vector", "s_aca", min(5 * (i + 2), 5 * NT))
            if i >= RS - 1:
                W("vector", "s_poo", 2 * (i - RS + 2))
            emit("vector", lambda e, a=bfv(K_QL, sl), b=bfv(K_R6L, sl), c2=bfv(K_MBL, sl): e.tensor_tensor(a[:], b[:], c2[:], OP.add))
            emit("vector", lambda e, a=bfv(K_QH, sl), b=bfv(K_R6H, sl), c2=bfv(K_MBH, sl): e.tensor_tensor(a[:], b[:], c2[:], OP.add))
            emit("vector", lambda e, a=bfv(K_Q2L, sl), b=bfv(K_QL, sl): e.tensor_scalar(a[:], b[:], -4.0, 0.0, OP.add, OP.max))
            emit("vector", lambda e, a=bfv(K_Q2H, sl), b=bfv(K_QH, sl): e.tensor_scalar(a[:], b[:], -4.0, 0.0, OP.add, OP.max))
            emit("vector", lambda e, a=bfv(K_R6L, sl), b=bfv(K_MBL, sl), c2=bfv(K_Q2L, sl): e.tensor_tensor(a[:], b[:], c2[:], OP.add))
            emit("vector", lambda e, a=bfv(K_QL, sl), b=bfv(K_R6L, sl), c2=bfv(K_SGL, sl): e.tensor_tensor(a[:], b[:], c2[:], OP.mult).then_inc(SEM["s_dvc"], 1))
            emit("vector", lambda e, a=bfv(K_R6H, sl), b=bfv(K_MBH, sl), c2=bfv(K_Q2H, sl): e.tensor_tensor(a[:], b[:], c2[:], OP.add))
            emit("vector", lambda e, a=bfv(K_QH, sl), b=bfv(K_R6H, sl), c2=bfv(K_SGH, sl): e.tensor_tensor(a[:], b[:], c2[:], OP.mult).then_inc(SEM["s_dvc"], 1))
            if i + RS < NT:
                emit_vecA(i + RS)
        cnt["s_dva"] = 3 * NT
        cnt["s_dvc"] = 2 * NT
        cnt["s_pre"] = 1

        def emit_scalA(i):
            sl = i
            pk = u8v(U8_PK, sl)
            W("scalar", "s_dva", min(3 * (i + 2), 3 * NT))
            W("scalar", "s_pk", min(16 * (i + 2), 16 * (NT + 1)))
            if i >= RS:
                W("scalar", "s_dvc", 2 * (i - RS) + 2)
            for fn_ in (
                lambda e, a=bfv(K_SGH, sl), b=pk: e.activation(a[:], b[:], AF.Sign, scale=-1.0, bias=cst[:, C_B1275:C_B1275 + 1]).then_inc(SEM["s_aca"], 1),
                lambda e, a=bfv(K_R6L, sl), b=u8v(U8_M8L, sl): e.activation(a[:], b[:], AF.Relu, scale=2.0, bias=cst[:, C_M12:C_M12 + 1]).then_inc(SEM["s_aca"], 1),
                lambda e, a=bfv(K_R6H, sl), b=u8v(U8_M8H, sl): e.activation(a[:], b[:], AF.Relu, scale=2.0, bias=cst[:, C_M12:C_M12 + 1]).then_inc(SEM["s_aca"], 1),
                lambda e, a=bfv(K_MBL, sl), b=u8v(U8_M8L, sl): e.activation(a[:], b[:], AF.Identity).then_inc(SEM["s_aca"], 1),
                lambda e, a=bfv(K_MBH, sl), b=u8v(U8_M8H, sl): e.activation(a[:], b[:], AF.Identity).then_inc(SEM["s_aca"], 1),
            ):
                emit("scalar", fn_)

        W("scalar", "s_ld", 48)
        for i in range(NT):
            emit_scalA(i)
            if i >= 1:
                lo_dst, hi_dst = tile_dsts(i - 1)
                psl = i - 1
                W("scalar", "s_poo", 2 * i)
                emit("scalar", lambda e, o=lo_dst, s2=bfv(K_VL, psl), s=SEM["s_wr"]: e.dma_start(out=o, in_=s2[:]).then_inc(s, 16))
                emit("scalar", lambda e, o=hi_dst, s2=bfv(K_VH, psl), s=SEM["s_wr"]: e.dma_start(out=o, in_=s2[:]).then_inc(s, 16))
        lo_dst, hi_dst = tile_dsts(NT - 1)
        W("scalar", "s_poo", 2 * NT)
        emit("scalar", lambda e, o=lo_dst, s2=bfv(K_VL, NT - 1), s=SEM["s_wr"]: e.dma_start(out=o, in_=s2[:]).then_inc(s, 16))
        emit("scalar", lambda e, o=hi_dst, s2=bfv(K_VH, NT - 1), s=SEM["s_wr"]: e.dma_start(out=o, in_=s2[:]).then_inc(s, 16))
        cnt["s_aca"] = 5 * NT
        cnt["s_wr"] = 32 * NT

        for k in range(min(2, NT)):
            _, scl_ap0 = tile_srcs(k)
            DMA("gpsimd", sclv(k)[:], scl_ap0, "s_scl")
        for i in range(NT):
            sl = i
            if i + 2 < NT:
                _, scl_ap2 = tile_srcs(i + 2)
                DMA("gpsimd", sclv(i + 2)[:], scl_ap2, "s_scl")
            elif i + 2 == NT:
                DMA("gpsimd", f32pad[:], gsT[0:1, 0:64], "s_scl")
            W("gpsimd", "s_dvc", 2 * i + 1)
            W("gpsimd", "s_scl", 16 * min(i + 2, NT + 1))
            if i >= RS:
                W("gpsimd", "s_wr", 32 * (i - RS + 1))
            emit("gpsimd", lambda e, a=bfv(K_VL, sl), b=bfv(K_QL, sl), c2=sclv(sl): e.tensor_tensor(a[:], b[:], c2[:], OP.mult).then_inc(SEM["s_poo"], 1))
            W("gpsimd", "s_dvc", 2 * i + 2)
            emit("gpsimd", lambda e, a=bfv(K_VH, sl), b=bfv(K_QH, sl), c2=sclv(sl): e.tensor_tensor(a[:], b[:], c2[:], OP.mult).then_inc(SEM["s_poo"], 1))
        cnt["s_poo"] = 2 * NT
        assert cnt["s_scl"] == 16 * (NT + 1)

        for eng in progs:
            W(eng, "s_wr", cnt["s_wr"])
        barrier()
        DBG("dbg_wg", wgS[0, :, 0:512], eng="gpsimd", conv=True)
        DBG("dbg_wd", wdS[:, 0, 0:512], eng="gpsimd", conv=True)
        if DEBUG_DUMPS and "dbg_wgS" in dbg:
            for nm2, src2 in (("dbg_wgS", wgS), ("dbg_wuS", wuS), ("dbg_wdS", wdS)):
                emit("gpsimd", lambda e, o=dbg[nm2], i=src2, s=SEM["s_st"]: e.dma_start(out=o, in_=i).then_inc(s, 16))
                INC("s_st", 16)

        def layer_norm(fold05):
            emit("vector", lambda e: e.tensor_copy(xbb_t[:, :, :], acc_t[:, :, :]))
            emit("vector", lambda e: e.tensor_tensor(sq_t[:, :, :], xbb_t[:, :, :], xbb_t[:, :, :], OP.mult).then_inc(SEM["s_stat"], 1))
            sv1 = INC("s_stat")
            W("tensor", "s_stat", sv1)
            W("tensor", "s_pre", 1)
            for kc in range(16):
                emit("tensor", lambda e, k=kc: e.matmul(ps_s1[:], ones_b[:], xbb_t[:, k, :], start=(k == 0), stop=(k == 15)))
            for kc in range(16):
                if kc < 15:
                    emit("tensor", lambda e, k=kc: e.matmul(ps_s2[:], ones_b[:], sq_t[:, k, :], start=(k == 0), stop=False))
                else:
                    emit("tensor", lambda e, k=kc: e.matmul(ps_s2[:], ones_b[:], sq_t[:, k, :], start=False, stop=True).then_inc(SEM["s_pa"], 1))
            pav = INC("s_pa")
            W("scalar", "s_pa", pav)
            emit("scalar", lambda e: e.activation(s_mean[:], ps_s1[:], AF.Identity, scale=1.0 / D).then_inc(SEM["s_stat"], 1))
            INC("s_stat")
            emit("scalar", lambda e: e.activation(s_ex2[:], ps_s2[:], AF.Identity, scale=1.0 / D).then_inc(SEM["s_stat"], 1))
            sv2 = INC("s_stat")
            W("vector", "s_stat", sv2)
            emit("vector", lambda e: e.tensor_tensor(s_mm2[:], s_mean[:], s_mean[:], OP.mult))
            emit("vector", lambda e: e.tensor_tensor(s_var[:], s_ex2[:], s_mm2[:], OP.subtract).then_inc(SEM["s_stat"], 1))
            sv3 = INC("s_stat")
            W("scalar", "s_stat", sv3)
            emit("scalar", lambda e: e.activation(s_lnv[:], s_var[:], AF.Ln, bias=cst[0:1, C_EPS:C_EPS + 1]))
            bc = C_LN05 if fold05 else C_ZERO
            emit("scalar", lambda e, b=bc: e.activation(s_rstd[:], s_lnv[:], AF.Exp, scale=-0.5, bias=cst[0:1, b:b + 1]).then_inc(SEM["s_stat"], 1))
            sv4 = INC("s_stat")
            W("sync", "s_stat", sv4)
            DMA("sync", statH[0:1, :], s_mean[:], "s_brd")
            hb = DMA("sync", statH[1:2, :], s_rstd[:], "s_brd")
            W("sync", "s_brd", hb)
            DMA("sync", mean_b[:], statH[0:1, :].unsqueeze(1).broadcast_to([1, 128, TB]), "s_brd")
            DMA("sync", rstd_b[:], statH[1:2, :].unsqueeze(1).broadcast_to([1, 128, TB]), "s_brd")
            brdv = cnt["s_brd"]
            W("vector", "s_brd", brdv)
            for kc in range(16):
                emit("vector", lambda e, k=kc: e.tensor_tensor(thalf_t[:, k % 8, :], acc_t[:, k, :], mean_b[:], OP.subtract))
                emit("vector", lambda e, k=kc: e.tensor_tensor(h_t[:, k, :], thalf_t[:, k % 8, :], rstd_b[:], OP.mult).then_inc(SEM["s_h"], 1))
            return INC("s_h", 16)

        for beta in range(NBLK):
            tsl = slice(TB * beta, TB * (beta + 1))

            W("sync", "s_st", cnt["s_st"])
            xv = DMA("sync", acc_t[:, :, :], xTs.rearrange("(c p) t -> p c t", p=128)[:, :, tsl], "s_x")
            W("gpsimd", "s_pg", cnt["s_pg"])
            W("gpsimd", "s_puu", cnt["s_puu"])
            DMA("gpsimd", bt_t[:, :, :], BTp.rearrange("(c p) n -> p c n", p=128), "s_bc")
            DMA("gpsimd", ct_t[:, :, :], CTp.rearrange("(c p) n -> p c n", p=128), "s_bc")
            bcv = cnt["s_bc"]
            W("vector", "s_x", xv)
            W("vector", "s_pd", cnt["s_pd"])
            W("vector", "s_pg", cnt["s_pg"])
            W("vector", "s_puu", cnt["s_puu"])

            if beta == 0:
                W("sync", "s_x", xv)
                DBG("dbg_acc0", acc_t[:, 0, :])
            hv1 = layer_norm(False)
            if beta == 0:
                W("gpsimd", "s_h", hv1)
                DBG("dbg_h1", h_t[:, 0, :], eng="gpsimd", conv=True)
                DBG("dbg_meanb", mean_b[:], eng="gpsimd")
                DBG("dbg_rstdb", rstd_b[:], eng="gpsimd")
                DBG("dbg_thalf", thalf_t[:, 0, :], eng="gpsimd")
                W("sync", "s_stat", cnt["s_stat"])
                DBG("dbg_mean", s_mean[:])
                DBG("dbg_rstd", s_rstd[:])

            ge_base = cnt["s_ge"]
            ev_base = cnt["s_ev"]
            W("tensor", "s_h", hv1)
            W("tensor", "s_bc", bcv)
            W("tensor", "s_ge", ge_base)
            tanh_ge = []
            for nsc in range(8):
                bank = ps_g[nsc % 2]
                if nsc >= 2:
                    W("tensor", "s_ge", tanh_ge[nsc - 2])
                for kc in range(16):
                    if kc < 15:
                        emit("tensor", lambda e, k=kc, n=nsc, b=bank: e.matmul(b[:], bt_t[:, k, 128 * n:128 * n + 128], h_t[:, k, :], start=(k == 0), stop=False))
                    else:
                        emit("tensor", lambda e, n=nsc, b=bank: e.matmul(b[:], bt_t[:, 15, 128 * n:128 * n + 128], h_t[:, 15, :], start=False, stop=True).then_inc(SEM["s_pa"], 1))
                pav = INC("s_pa")
                W("scalar", "s_pa", pav)
                emit("scalar", lambda e, n=nsc, b=bank: e.activation(stb_t[:, n, :], b[:], AF.Tanh).then_inc(SEM["s_ge"], 1))
                tanh_ge.append(INC("s_ge"))
            attn_ev = []
            W("tensor", "s_ev", ev_base)
            for dc in range(16):
                bank = ps_d[dc % 2]
                W("tensor", "s_ge", tanh_ge[7])
                if dc >= 2:
                    W("tensor", "s_ev", attn_ev[dc - 2])
                for nsc in range(8):
                    if nsc < 7:
                        emit("tensor", lambda e, n=nsc, d2=dc, b=bank: e.matmul(b[:], ct_t[:, n, 128 * d2:128 * d2 + 128], stb_t[:, n, :], start=(n == 0), stop=False))
                    else:
                        emit("tensor", lambda e, d2=dc, b=bank: e.matmul(b[:], ct_t[:, 7, 128 * d2:128 * d2 + 128], stb_t[:, 7, :], start=False, stop=True).then_inc(SEM["s_pd"], 1))
                pdv = INC("s_pd")
                W("vector", "s_pd", pdv)
                emit("vector", lambda e, d2=dc, b=bank: e.tensor_tensor(acc_t[:, d2, :], acc_t[:, d2, :], b[:], OP.add))
                emit("vector", lambda e, d2=dc: e.scalar_tensor_tensor(acc_t[:, d2, :], h_t[:, d2, :], dv[:, d2:d2 + 1], acc_t[:, d2, :], OP.mult, OP.add).then_inc(SEM["s_ev"], 1))
                attn_ev.append(INC("s_ev"))

            if beta == 0:
                W("gpsimd", "s_ge", tanh_ge[7])
                DBG("dbg_stb", stb_t[:, 0, :], eng="gpsimd", conv=True)
                W("sync", "s_ev", attn_ev[15])
                DBG("dbg_acc1", acc_t[:, 0, :])

            W("vector", "s_ev", attn_ev[15])
            hv2 = layer_norm(True)

            pa_attn = cnt["s_pa"]
            pd_attn = cnt["s_pd"]
            it_base = cnt["s_it"]
            if beta == 0:
                W("gpsimd", "s_h", hv2)
                DBG("dbg_h2", h_t[:, 0, :], eng="gpsimd", conv=True)
            W("tensor", "s_h", hv2)
            unit_gate = []
            unit_up = []
            grp_pd = []
            gelu_ge = []
            it_h = []
            dn_ev = []
            for j in range(NUNIT):
                nsl = slice(TB * j, TB * (j + 1))
                if j < 2:
                    W("sync", "s_pa", pa_attn)
                    W("sync", "s_pd", pd_attn)
                    W("sync", "s_ev", attn_ev[15])
                else:
                    W("sync", "s_pg", unit_gate[j - 2])
                    W("sync", "s_puu", unit_up[j - 2])
                    gprev = (j - 2) // 2
                    if gprev < len(grp_pd):
                        W("sync", "s_pd", grp_pd[gprev])
                DMA("sync", wg_u(j % 2)[:, :, :], wgS[:, :, nsl].rearrange("k p n -> p k n"), "s_ul")
                DMA("sync", wu_u(j % 2)[:, :, :], wuS[:, :, nsl].rearrange("k p n -> p k n"), "s_ul")
                ulv = DMA("sync", wd_u(j % 2)[:, :, :], wdS[:, 4 * j:4 * j + 4, :], "s_ul")

                W("tensor", "s_ul", ulv)
                for ep in range(4):
                    e_g = 4 * j + ep
                    gb, ub = ps_g[e_g % 2], ps_u[e_g % 2]
                    W("tensor", "s_ge", gelu_ge[e_g - 2] if e_g >= 2 else tanh_ge[7])
                    for kk in range(16):
                        if kk < 15:
                            emit("tensor", lambda e, k=kk, p=ep, b=gb, j2=j: e.matmul(b[:], wg_u(j2 % 2)[:, k, 128 * p:128 * p + 128], h_t[:, k, :], start=(k == 0), stop=False))
                        else:
                            emit("tensor", lambda e, p=ep, b=gb, j2=j: e.matmul(b[:], wg_u(j2 % 2)[:, 15, 128 * p:128 * p + 128], h_t[:, 15, :], start=False, stop=True).then_inc(SEM["s_pg"], 1))
                    pgv = INC("s_pg")
                    W("tensor", "s_it", it_h[e_g - 2] if e_g >= 2 else it_base)
                    for kk in range(16):
                        if kk < 15:
                            emit("tensor", lambda e, k=kk, p=ep, b=ub, j2=j: e.matmul(b[:], wu_u(j2 % 2)[:, k, 128 * p:128 * p + 128], h_t[:, k, :], start=(k == 0), stop=False))
                        else:
                            emit("tensor", lambda e, p=ep, b=ub, j2=j: e.matmul(b[:], wu_u(j2 % 2)[:, 15, 128 * p:128 * p + 128], h_t[:, 15, :], start=False, stop=True).then_inc(SEM["s_puu"], 1))
                    puv = INC("s_puu")
                    W("scalar", "s_pg", pgv)
                    W("scalar", "s_it", it_h[e_g - 2] if e_g >= 2 else it_base)
                    emit("scalar", lambda e, b=gb, g=g1_t(e_g % 2): e.activation(g[:], b[:], AF.Gelu_apprx_tanh).then_inc(SEM["s_ge"], 1))
                    gelu_ge.append(INC("s_ge"))
                    W("vector", "s_ge", gelu_ge[e_g])
                    W("vector", "s_puu", puv)
                    if e_g >= GSZ:
                        W("vector", "s_pd", grp_pd[e_g // GSZ - 1])
                    emit("vector", lambda e, b=ub, g=g1_t(e_g % 2), s2=e_g % GSZ: e.scalar_tensor_tensor(inter_t[:, s2, :], g[:], 0.5, b[:], OP.mult, OP.mult).then_inc(SEM["s_it"], 1))
                    it_h.append(INC("s_it"))
                if beta == 0 and j == 0:
                    W("gpsimd", "s_it", it_h[0])
                    DBG("dbg_inter", inter_t[:, 0, :], eng="gpsimd", conv=True)
                unit_gate.append(cnt["s_pg"])
                unit_up.append(cnt["s_puu"])

                if j % 2 == 1:
                    g = j // 2
                    W("tensor", "s_it", it_h[GSZ * g + GSZ - 1])
                    for hc in range(16):
                        db = ps_d[hc % 2]
                        li = 16 * g + hc
                        if li >= 2:
                            W("tensor", "s_ev", dn_ev[li - 2])
                        else:
                            W("tensor", "s_ev", attn_ev[15])
                        for ei in range(GSZ):
                            e_g = GSZ * g + ei
                            jj = e_g // 4
                            if ei < GSZ - 1:
                                emit("tensor", lambda e, j2=jj, t=e_g % 4, h2=hc, s2=e_g % GSZ, b=db: e.matmul(b[:], wd_u(j2 % 2)[:, t, 128 * h2:128 * h2 + 128], inter_t[:, s2, :], start=(s2 == 0), stop=False))
                            else:
                                emit("tensor", lambda e, j2=jj, t=e_g % 4, h2=hc, s2=e_g % GSZ, b=db: e.matmul(b[:], wd_u(j2 % 2)[:, t, 128 * h2:128 * h2 + 128], inter_t[:, s2, :], start=False, stop=True).then_inc(SEM["s_pd"], 1))
                        pdv = INC("s_pd")
                        W("vector", "s_pd", pdv)
                        emit("vector", lambda e, h2=hc, b=db: e.tensor_tensor(acc_t[:, h2, :], acc_t[:, h2, :], b[:], OP.add).then_inc(SEM["s_ev"], 1))
                        dn_ev.append(INC("s_ev"))
                    grp_pd.append(cnt["s_pd"])

            W("scalar", "s_ev", dn_ev[-1])
            DMA("scalar", oT.rearrange("(c p) t -> p c t", p=128)[:, :, tsl], acc_t[:, :, :], "s_st")
            barrier()

        W("gpsimd", "s_st", cnt["s_st"])

        with nc.Block() as block:
            @block.sync
            def _(eng):
                for fn in progs["sync"]:
                    fn(eng)

            @block.gpsimd
            def _(eng):
                for fn in progs["gpsimd"]:
                    fn(eng)

            @block.vector
            def _(eng):
                for fn in progs["vector"]:
                    fn(eng)

            @block.scalar
            def _(eng):
                for fn in progs["scalar"]:
                    fn(eng)

            @block.tensor
            def _(eng):
                for fn in progs["tensor"]:
                    fn(eng)

    return nc


def kernel(L0_attn_x, L0_attn_B, L0_attn_C, L0_attn_D,
           L0_ffn_gate_W_packed, L0_ffn_gate_scales,
           L0_ffn_up_W_packed, L0_ffn_up_scales,
           L0_ffn_down_W_packed, L0_ffn_down_scales):
    pd_ = _perm(D)
    pn = _perm(NM)

    x = np.ascontiguousarray(np.asarray(L0_attn_x, np.float32).reshape(-1, D).T[pd_])
    BTp = np.ascontiguousarray(np.asarray(L0_attn_B, np.float32).T[pd_])
    CTp = np.ascontiguousarray(np.asarray(L0_attn_C, np.float32).T[:, pd_])
    Dp = np.ascontiguousarray(np.asarray(L0_attn_D, np.float32)[pd_].reshape(16, 128).T)

    def pack_u8(Wp):
        w = np.ascontiguousarray(np.asarray(Wp))
        return w.view(np.uint8).reshape(w.shape[0], w.shape[1], 4)[:, :, 0]

    gpTh = np.ascontiguousarray(pack_u8(L0_ffn_gate_W_packed).T[:, pn])
    upTh = np.ascontiguousarray(pack_u8(L0_ffn_up_W_packed).T[:, pn])
    dpTh = np.ascontiguousarray(pack_u8(L0_ffn_down_W_packed).T[:, pd_])
    gsTh = np.ascontiguousarray(np.asarray(L0_ffn_gate_scales, np.float32).T[:, pn])
    usTh = np.ascontiguousarray(np.asarray(L0_ffn_up_scales, np.float32).T[:, pn])
    dsTh = np.ascontiguousarray(np.asarray(L0_ffn_down_scales, np.float32).T[:, pd_])

    cstv = np.zeros((128, 8), np.float32)
    cstv[:, C_M12] = -12.0
    cstv[:, C_M4] = -4.0
    cstv[:, C_B4] = 4.0
    cstv[:, C_B1275] = 127.5
    cstv[:, C_EPS] = 1e-5
    cstv[:, C_LN05] = LN05
    ones = np.ones((128, 1), np.float32)

    nc = build_program()
    in_maps = []
    for r in range(NCORES):
        in_maps.append(dict(
            xTs=np.ascontiguousarray(x[:, TPC * r:TPC * (r + 1)]),
            gpT=gpTh, upT=upTh, dpT=dpTh, gsT=gsTh, usT=usTh, dsT=dsTh,
            BTp=BTp, CTp=CTp, Dp=Dp, cstI=cstv, onesI=ones,
        ))
    res = run_bass_kernel_spmd(nc, in_maps, core_ids=list(range(NCORES)))
    outT = np.concatenate([r["oT"] for r in res.results], axis=1)
    inv = np.argsort(pd_)
    out = outT[inv].T.reshape(2, 8192, D)
    return np.ascontiguousarray(out, dtype=np.float32)


# version 06
# speedup vs baseline: 1.2348x; 1.2348x over previous
import contextlib
import numpy as np
import concourse.bass as bass
import concourse.mybir as mybir
from concourse.bass_utils import run_bass_kernel_spmd

F32 = mybir.dt.float32
BF16 = mybir.dt.bfloat16
U8 = mybir.dt.uint8
AF = mybir.ActivationFunctionType
OP = mybir.AluOpType

NCORES = 8
D = 2048
NS = 1024
NM = 16384
TPC = 2048
TB = 512
NBLK = TPC // TB
FD = 1024
RS = 4
NGT = NM // FD
NUNIT = NM // TB
ECH = NM // 128
GSZ = 8
LN05 = float(np.log(0.5))

C_M12, C_M4, C_B4, C_B1275, C_EPS, C_LN05, C_ZERO, C_SP = range(8)

DEBUG_DUMPS = False


def _perm(n):
    blk = np.concatenate([np.arange(0, 256, 2), np.arange(1, 256, 2)])
    return (np.arange(n // 256)[:, None] * 256 + blk[None, :]).ravel()


def build_program():
    nc = bass.Bass("TRN2", target_bir_lowering=False)

    xTs = nc.dram_tensor("xTs", [D, TPC], F32, kind="ExternalInput").ap()
    gpT = nc.dram_tensor("gpT", [1024, NM], U8, kind="ExternalInput").ap()
    upT = nc.dram_tensor("upT", [1024, NM], U8, kind="ExternalInput").ap()
    dpT = nc.dram_tensor("dpT", [8192, D], U8, kind="ExternalInput").ap()
    gsT = nc.dram_tensor("gsT", [64, NM], F32, kind="ExternalInput").ap()
    usT = nc.dram_tensor("usT", [64, NM], F32, kind="ExternalInput").ap()
    dsT = nc.dram_tensor("dsT", [512, D], F32, kind="ExternalInput").ap()
    BTp = nc.dram_tensor("BTp", [D, NS], F32, kind="ExternalInput").ap()
    CTp = nc.dram_tensor("CTp", [NS, D], F32, kind="ExternalInput").ap()
    Dp = nc.dram_tensor("Dp", [128, 16], F32, kind="ExternalInput").ap()
    cstI = nc.dram_tensor("cstI", [128, 8], F32, kind="ExternalInput").ap()
    onesI = nc.dram_tensor("onesI", [128, 1], F32, kind="ExternalInput").ap()
    oT = nc.dram_tensor("oT", [D, TPC], F32, kind="ExternalOutput").ap()
    dbg = {}
    if DEBUG_DUMPS:
        import os
        if os.environ.get("DBG_FULL_SCRATCH"):
            for nm, shp, dt2 in (("dbg_wgS", [16, 128, NM], BF16),
                                 ("dbg_wuS", [16, 128, NM], BF16),
                                 ("dbg_wdS", [128, ECH, D], BF16)):
                dbg[nm] = nc.dram_tensor(nm, shp, dt2, kind="ExternalOutput").ap()
        for nm, shp in (("dbg_wg", [128, 512]), ("dbg_wd", [128, 512]),
                        ("dbg_acc0", [128, 512]), ("dbg_h1", [128, 512]),
                        ("dbg_mean", [1, 512]), ("dbg_rstd", [1, 512]),
                        ("dbg_stb", [128, 512]), ("dbg_acc1", [128, 512]),
                        ("dbg_meanb", [128, 512]), ("dbg_rstdb", [128, 512]),
                        ("dbg_thalf", [128, 512]),
                        ("dbg_h2", [128, 512]), ("dbg_inter", [128, 512])):
            dbg[nm] = nc.dram_tensor(nm, shp, F32, kind="ExternalOutput").ap()

    wgS = nc.dram_tensor("wgS", [16, 128, NM], BF16, kind="Internal").ap()
    wuS = nc.dram_tensor("wuS", [16, 128, NM], BF16, kind="Internal").ap()
    wdS = nc.dram_tensor("wdS", [128, ECH, D], BF16, kind="Internal").ap()
    statH = nc.dram_tensor("statH", [2, TB], F32, kind="Internal").ap()

    ctx = contextlib.ExitStack()
    with ctx as st_:
        def sbt(name, shape, dt):
            return st_.enter_context(nc.sbuf_tensor(name, shape, dt))

        arU8 = sbt("arU8", [128, 5 * RS * FD], U8)
        arBF = sbt("arBF", [128, 49152], BF16)
        h_sb = sbt("h_sb", [128, 8192], BF16)
        inter_sb = sbt("inter_sb", [128, 4096], BF16)
        g1_sb = sbt("g1_sb", [128, 2 * TB], BF16)
        arF32 = sbt("arF32", [128, 13312], F32)
        u8pad = sbt("u8pad", [1, 64], U8)
        f32pad = sbt("f32pad", [1, 64], F32)
        cst = sbt("cst", [128, 8], F32)
        dv = sbt("dv", [128, 16], F32)
        ones_f = sbt("ones_f", [128, 1], F32)
        ones_b = sbt("ones_b", [128, 1], BF16)
        s_mean = sbt("s_mean", [1, TB], F32)
        s_ex2 = sbt("s_ex2", [1, TB], F32)
        s_mm2 = sbt("s_mm2", [1, TB], F32)
        s_var = sbt("s_var", [1, TB], F32)
        s_lnv = sbt("s_lnv", [1, TB], F32)
        s_rstd = sbt("s_rstd", [1, TB], F32)

        ps_g = [st_.enter_context(nc.psum_tensor(f"ps_g{i}", [128, TB], F32)) for i in range(2)]
        ps_u = [st_.enter_context(nc.psum_tensor(f"ps_u{i}", [128, TB], F32)) for i in range(2)]
        ps_d = [st_.enter_context(nc.psum_tensor(f"ps_d{i}", [128, TB], F32)) for i in range(2)]
        ps_s1 = st_.enter_context(nc.psum_tensor("ps_s1", [1, TB], F32))
        ps_s2 = st_.enter_context(nc.psum_tensor("ps_s2", [1, TB], F32))

        semnames = ["s_ld", "s_pk", "s_scl", "s_dva", "s_dvb", "s_dvc",
                    "s_aca", "s_acb", "s_poo", "s_wr", "s_x", "s_bc",
                    "s_stat", "s_brd", "s_h", "s_pg", "s_puu", "s_pd",
                    "s_pa", "s_ge", "s_it", "s_ev", "s_ul", "s_st", "s_bar",
                    "s_pre"]
        SEM = {n: st_.enter_context(nc.semaphore(n)) for n in semnames}

        U8_PK, U8_T8, U8_HI8, U8_M8L, U8_M8H = (i * RS * FD for i in range(5))

        def u8v(base, slot):
            o = base + (slot % RS) * FD
            return arU8[:, o:o + FD]

        (K_MBL, K_MBH, K_SGL, K_SGH, K_R6L, K_R6H,
         K_QL, K_QH, K_Q2L, K_Q2H, K_VL, K_VH) = range(12)

        def bfv(kind, slot):
            o = (kind * RS + (slot % RS)) * FD
            return arBF[:, o:o + FD]

        bt_t = arBF[:, 0:16384].rearrange("p (k n) -> p k n", k=16)
        ct_t = arBF[:, 16384:32768].rearrange("p (k n) -> p k n", k=8)
        xbb_t = arBF[:, 32768:40960].rearrange("p (k t) -> p k t", k=16)
        sq_t = arBF[:, 40960:49152].rearrange("p (k t) -> p k t", k=16)
        stb_t = arBF[:, 32768:36864].rearrange("p (k t) -> p k t", k=8)

        def wg_u(slot):
            return arBF[:, slot * 8192:(slot + 1) * 8192].rearrange("p (k n) -> p k n", k=16)

        def wu_u(slot):
            o = 16384 + slot * 8192
            return arBF[:, o:o + 8192].rearrange("p (k n) -> p k n", k=16)

        def wd_u(slot):
            o = 32768 + slot * 8192
            return arBF[:, o:o + 8192].rearrange("p (e h) -> p e h", e=4)

        h_t = h_sb[:].rearrange("p (k t) -> p k t", k=16)
        inter_t = inter_sb[:].rearrange("p (e t) -> p e t", e=8)

        def g1_t(slot):
            return g1_sb[:, slot * TB:(slot + 1) * TB]

        acc_t = arF32[:, 0:8192].rearrange("p (k t) -> p k t", k=16)

        def sclv(slot):
            o = 8192 + (slot % RS) * FD
            return arF32[:, o:o + FD]

        thalf_t = arF32[:, 8192:12288].rearrange("p (k t) -> p k t", k=8)
        mean_b = arF32[:, 12288:12800]
        rstd_b = arF32[:, 12800:13312]

        progs = {e: [] for e in ("sync", "gpsimd", "vector", "scalar", "tensor")}
        cnt = {n: 0 for n in semnames}
        nbar = [0]

        def emit(eng, fn):
            progs[eng].append(fn)

        def W(eng, sem, val):
            if val > 0:
                emit(eng, lambda e, s=SEM[sem], v=val: e.wait_ge(s, v))

        def INC(sem, k=1):
            cnt[sem] += k
            return cnt[sem]

        def DMA(eng, out, in_, sem):
            v = INC(sem, 16)
            emit(eng, lambda e, o=out, i=in_, s=SEM[sem]: e.dma_start(out=o, in_=i).then_inc(s, 16))
            return v

        def barrier():
            nbar[0] += 1
            tgt = 5 * nbar[0]
            for eng in progs:
                emit(eng, lambda e, s=SEM["s_bar"]: e.nop().then_inc(s, 1))
                emit(eng, lambda e, s=SEM["s_bar"], v=tgt: e.wait_ge(s, v))


        def DBG(name, src_ap, eng="sync", conv=False):
            if not DEBUG_DUMPS or name not in dbg:
                return
            dbg_f = arF32[:, 8192:8704]
            if conv:
                v = INC("s_ld", 16)
                emit(eng, lambda e, o=dbg_f, i=src_ap, s=SEM["s_ld"]: e.dma_start(out=o, in_=i).then_inc(s, 16))
                W(eng, "s_ld", v)
                emit(eng, lambda e, o=dbg[name], i=dbg_f, s=SEM["s_st"]: e.dma_start(out=o, in_=i).then_inc(s, 16))
            else:
                emit(eng, lambda e, o=dbg[name], i=src_ap, s=SEM["s_st"]: e.dma_start(out=o, in_=i).then_inc(s, 16))
            INC("s_st", 16)

        DMA("sync", cst[:], cstI, "s_ld")
        DMA("sync", dv[:], Dp, "s_ld")
        DMA("sync", ones_f[:], onesI, "s_ld")
        W("vector", "s_ld", 48)
        emit("vector", lambda e: e.tensor_copy(ones_b[:], ones_f[:]).then_inc(SEM["s_pre"], 1))
        INC("s_pre")

        tiles = []
        for psrc, ssrc, dst in ((gpT, gsT, wgS), (upT, usT, wuS)):
            for nr in range(NGT):
                for c in range(8):
                    tiles.append(("gu", psrc, ssrc, dst, c, nr))
        for cd in range(64):
            for hr in range(2):
                tiles.append(("d", dpT, dsT, wdS, cd, hr))
        NT = len(tiles)

        def tile_srcs(i):
            kind, psrc, ssrc, dst, c, nr = tiles[i]
            pk_ap = psrc[128 * c:128 * (c + 1), FD * nr:FD * (nr + 1)]
            scl_ap = (ssrc[8 * c:8 * c + 8, FD * nr:FD * (nr + 1)]
                      .unsqueeze(1).broadcast_to([8, 16, FD]))
            return pk_ap, scl_ap

        def tile_dsts(i):
            kind, psrc, ssrc, dst, c, nr = tiles[i]
            if kind == "gu":
                return (dst[2 * c, :, FD * nr:FD * (nr + 1)],
                        dst[2 * c + 1, :, FD * nr:FD * (nr + 1)])
            return (dst[:, 2 * c, FD * nr:FD * (nr + 1)],
                    dst[:, 2 * c + 1, FD * nr:FD * (nr + 1)])

        for i in range(NT):
            pk_ap, _ = tile_srcs(i)
            if i >= RS:
                W("sync", "s_aca", 5 * (i - RS + 1))
                W("sync", "s_dva", 3 * (i - RS + 1))
            DMA("sync", u8v(U8_PK, i)[:], pk_ap, "s_pk")
        DMA("sync", u8pad[:], gpT[0:1, 0:64], "s_pk")
        DMA("sync", u8pad[:], gpT[0:1, 64:128], "s_pk")
        assert cnt["s_pk"] == 16 * (NT + 2)

        def emit_vecA(i):
            sl = i
            pk = u8v(U8_PK, sl)
            W("vector", "s_pk", 16 * min(i + 3, NT + 2))
            emit("vector", lambda e, a=u8v(U8_T8, sl), b=pk: e.tensor_scalar(a[:], b[:], 8, None, OP.bitwise_and).then_inc(SEM["s_dva"], 1))
            emit("vector", lambda e, a=u8v(U8_HI8, sl), b=pk: e.tensor_scalar(a[:], b[:], 4, None, OP.logical_shift_right))
            emit("vector", lambda e, a=u8v(U8_M8L, sl), b=pk: e.tensor_scalar(a[:], b[:], 7, None, OP.bitwise_and).then_inc(SEM["s_dva"], 1))
            emit("vector", lambda e, a=u8v(U8_M8H, sl), b=u8v(U8_HI8, sl): e.tensor_scalar(a[:], b[:], 7, None, OP.bitwise_and).then_inc(SEM["s_dva"], 1))
            emit("vector", lambda e, a=bfv(K_SGL, sl), b=u8v(U8_T8, sl): e.tensor_scalar(a[:], b[:], -0.25, 1.0, OP.mult, OP.add))

        for k in range(min(RS, NT)):
            emit_vecA(k)
        for i in range(NT):
            sl = i
            W("vector", "s_aca", min(5 * (i + 2), 5 * NT))
            if i >= RS - 1:
                W("vector", "s_poo", i - RS + 2)
            if i >= RS:
                W("vector", "s_wr", 32 * (i - RS + 1))
            emit("vector", lambda e, a=bfv(K_QL, sl), b=bfv(K_R6L, sl), c2=bfv(K_MBL, sl): e.tensor_tensor(a[:], b[:], c2[:], OP.add))
            emit("vector", lambda e, a=bfv(K_QH, sl), b=bfv(K_R6H, sl), c2=bfv(K_MBH, sl): e.tensor_tensor(a[:], b[:], c2[:], OP.add))
            emit("vector", lambda e, a=bfv(K_Q2L, sl), b=bfv(K_QL, sl): e.tensor_scalar(a[:], b[:], -4.0, 0.0, OP.add, OP.max))
            emit("vector", lambda e, a=bfv(K_Q2H, sl), b=bfv(K_QH, sl): e.tensor_scalar(a[:], b[:], -4.0, 0.0, OP.add, OP.max))
            emit("vector", lambda e, a=bfv(K_R6L, sl), b=bfv(K_MBL, sl), c2=bfv(K_Q2L, sl): e.tensor_tensor(a[:], b[:], c2[:], OP.add))
            emit("vector", lambda e, a=bfv(K_R6H, sl), b=bfv(K_MBH, sl), c2=bfv(K_Q2H, sl): e.tensor_tensor(a[:], b[:], c2[:], OP.add))
            emit("vector", lambda e, a=bfv(K_QH, sl), b=bfv(K_R6H, sl), c2=bfv(K_SGH, sl): e.tensor_tensor(a[:], b[:], c2[:], OP.mult).then_inc(SEM["s_dvc"], 1))
            W("vector", "s_scl", 16 * min(i + 3, NT + 2))
            emit("vector", lambda e, a=bfv(K_VL, sl), b=bfv(K_R6L, sl), c2=bfv(K_SGL, sl): e.tensor_tensor(a[:], b[:], c2[:], OP.mult))
            emit("vector", lambda e, a=bfv(K_VL, sl), b=bfv(K_VL, sl), c2=sclv(sl): e.tensor_tensor(a[:], b[:], c2[:], OP.mult).then_inc(SEM["s_dvb"], 1))
            if i + RS < NT:
                emit_vecA(i + RS)
        cnt["s_dva"] = 3 * NT
        cnt["s_dvc"] = NT
        cnt["s_dvb"] = NT
        cnt["s_pre"] = 1

        def emit_scalA(i):
            sl = i
            pk = u8v(U8_PK, sl)
            W("scalar", "s_dva", min(3 * (i + 2), 3 * NT))
            W("scalar", "s_pk", 16 * min(i + 3, NT + 2))
            if i >= RS:
                W("scalar", "s_dvc", i - RS + 1)
                W("scalar", "s_dvb", i - RS + 1)
            for fn_ in (
                lambda e, a=bfv(K_SGH, sl), b=pk: e.activation(a[:], b[:], AF.Sign, scale=-1.0, bias=cst[:, C_B1275:C_B1275 + 1]).then_inc(SEM["s_aca"], 1),
                lambda e, a=bfv(K_R6L, sl), b=u8v(U8_M8L, sl): e.activation(a[:], b[:], AF.Relu, scale=2.0, bias=cst[:, C_M12:C_M12 + 1]).then_inc(SEM["s_aca"], 1),
                lambda e, a=bfv(K_R6H, sl), b=u8v(U8_M8H, sl): e.activation(a[:], b[:], AF.Relu, scale=2.0, bias=cst[:, C_M12:C_M12 + 1]).then_inc(SEM["s_aca"], 1),
                lambda e, a=bfv(K_MBL, sl), b=u8v(U8_M8L, sl): e.activation(a[:], b[:], AF.Identity).then_inc(SEM["s_aca"], 1),
                lambda e, a=bfv(K_MBH, sl), b=u8v(U8_M8H, sl): e.activation(a[:], b[:], AF.Identity).then_inc(SEM["s_aca"], 1),
            ):
                emit("scalar", fn_)

        W("scalar", "s_ld", 48)
        for i in range(NT):
            emit_scalA(i)
            if i >= 2:
                lo_dst, hi_dst = tile_dsts(i - 2)
                psl = i - 2
                W("scalar", "s_dvb", i - 1)
                W("scalar", "s_poo", i - 1)
                emit("scalar", lambda e, o=lo_dst, s2=bfv(K_VL, psl), s=SEM["s_wr"]: e.dma_start(out=o, in_=s2[:]).then_inc(s, 16))
                emit("scalar", lambda e, o=hi_dst, s2=bfv(K_VH, psl), s=SEM["s_wr"]: e.dma_start(out=o, in_=s2[:]).then_inc(s, 16))
        for j in (NT - 2, NT - 1):
            lo_dst, hi_dst = tile_dsts(j)
            W("scalar", "s_dvb", j + 1)
            W("scalar", "s_poo", j + 1)
            emit("scalar", lambda e, o=lo_dst, s2=bfv(K_VL, j), s=SEM["s_wr"]: e.dma_start(out=o, in_=s2[:]).then_inc(s, 16))
            emit("scalar", lambda e, o=hi_dst, s2=bfv(K_VH, j), s=SEM["s_wr"]: e.dma_start(out=o, in_=s2[:]).then_inc(s, 16))
        cnt["s_aca"] = 5 * NT
        cnt["s_wr"] = 32 * NT

        for k in range(min(3, NT)):
            _, scl_ap0 = tile_srcs(k)
            DMA("gpsimd", sclv(k)[:], scl_ap0, "s_scl")
        for i in range(NT):
            sl = i
            if i + 3 < NT:
                _, scl_ap2 = tile_srcs(i + 3)
                if i >= 1:
                    W("gpsimd", "s_dvb", i)
                DMA("gpsimd", sclv(i + 3)[:], scl_ap2, "s_scl")
            elif i + 3 in (NT, NT + 1):
                DMA("gpsimd", f32pad[:], gsT[0:1, 0:64], "s_scl")
            W("gpsimd", "s_dvc", i + 1)
            W("gpsimd", "s_scl", 16 * min(i + 3, NT + 2))
            if i >= RS:
                W("gpsimd", "s_wr", 32 * (i - RS + 1))
            emit("gpsimd", lambda e, a=bfv(K_VH, sl), b=bfv(K_QH, sl), c2=sclv(sl): e.tensor_tensor(a[:], b[:], c2[:], OP.mult).then_inc(SEM["s_poo"], 1))
        cnt["s_poo"] = NT
        assert cnt["s_scl"] == 16 * (NT + 2)

        for eng in progs:
            W(eng, "s_wr", cnt["s_wr"])
        barrier()
        DBG("dbg_wg", wgS[0, :, 0:512], eng="gpsimd", conv=True)
        DBG("dbg_wd", wdS[:, 0, 0:512], eng="gpsimd", conv=True)
        if DEBUG_DUMPS and "dbg_wgS" in dbg:
            for nm2, src2 in (("dbg_wgS", wgS), ("dbg_wuS", wuS), ("dbg_wdS", wdS)):
                emit("gpsimd", lambda e, o=dbg[nm2], i=src2, s=SEM["s_st"]: e.dma_start(out=o, in_=i).then_inc(s, 16))
                INC("s_st", 16)

        def layer_norm(fold05):
            emit("vector", lambda e: e.tensor_copy(xbb_t[:, :, :], acc_t[:, :, :]))
            emit("vector", lambda e: e.tensor_tensor(sq_t[:, :, :], xbb_t[:, :, :], xbb_t[:, :, :], OP.mult).then_inc(SEM["s_stat"], 1))
            sv1 = INC("s_stat")
            W("tensor", "s_stat", sv1)
            W("tensor", "s_pre", 1)
            for kc in range(16):
                emit("tensor", lambda e, k=kc: e.matmul(ps_s1[:], ones_b[:], xbb_t[:, k, :], start=(k == 0), stop=(k == 15)))
            for kc in range(16):
                if kc < 15:
                    emit("tensor", lambda e, k=kc: e.matmul(ps_s2[:], ones_b[:], sq_t[:, k, :], start=(k == 0), stop=False))
                else:
                    emit("tensor", lambda e, k=kc: e.matmul(ps_s2[:], ones_b[:], sq_t[:, k, :], start=False, stop=True).then_inc(SEM["s_pa"], 1))
            pav = INC("s_pa")
            W("scalar", "s_pa", pav)
            emit("scalar", lambda e: e.activation(s_mean[:], ps_s1[:], AF.Identity, scale=1.0 / D).then_inc(SEM["s_stat"], 1))
            INC("s_stat")
            emit("scalar", lambda e: e.activation(s_ex2[:], ps_s2[:], AF.Identity, scale=1.0 / D).then_inc(SEM["s_stat"], 1))
            sv2 = INC("s_stat")
            W("vector", "s_stat", sv2)
            emit("vector", lambda e: e.tensor_tensor(s_mm2[:], s_mean[:], s_mean[:], OP.mult))
            emit("vector", lambda e: e.tensor_tensor(s_var[:], s_ex2[:], s_mm2[:], OP.subtract).then_inc(SEM["s_stat"], 1))
            sv3 = INC("s_stat")
            W("scalar", "s_stat", sv3)
            emit("scalar", lambda e: e.activation(s_lnv[:], s_var[:], AF.Ln, bias=cst[0:1, C_EPS:C_EPS + 1]))
            bc = C_LN05 if fold05 else C_ZERO
            emit("scalar", lambda e, b=bc: e.activation(s_rstd[:], s_lnv[:], AF.Exp, scale=-0.5, bias=cst[0:1, b:b + 1]).then_inc(SEM["s_stat"], 1))
            sv4 = INC("s_stat")
            W("sync", "s_stat", sv4)
            DMA("sync", statH[0:1, :], s_mean[:], "s_brd")
            hb = DMA("sync", statH[1:2, :], s_rstd[:], "s_brd")
            W("sync", "s_brd", hb)
            DMA("sync", mean_b[:], statH[0:1, :].unsqueeze(1).broadcast_to([1, 128, TB]), "s_brd")
            DMA("sync", rstd_b[:], statH[1:2, :].unsqueeze(1).broadcast_to([1, 128, TB]), "s_brd")
            brdv = cnt["s_brd"]
            W("vector", "s_brd", brdv)
            for kc in range(16):
                emit("vector", lambda e, k=kc: e.tensor_tensor(thalf_t[:, k % 8, :], acc_t[:, k, :], mean_b[:], OP.subtract))
                emit("vector", lambda e, k=kc: e.tensor_tensor(h_t[:, k, :], thalf_t[:, k % 8, :], rstd_b[:], OP.mult).then_inc(SEM["s_h"], 1))
            return INC("s_h", 16)

        for beta in range(NBLK):
            tsl = slice(TB * beta, TB * (beta + 1))

            W("sync", "s_st", cnt["s_st"])
            xv = DMA("sync", acc_t[:, :, :], xTs.rearrange("(c p) t -> p c t", p=128)[:, :, tsl], "s_x")
            W("gpsimd", "s_pg", cnt["s_pg"])
            W("gpsimd", "s_puu", cnt["s_puu"])
            DMA("gpsimd", bt_t[:, :, :], BTp.rearrange("(c p) n -> p c n", p=128), "s_bc")
            DMA("gpsimd", ct_t[:, :, :], CTp.rearrange("(c p) n -> p c n", p=128), "s_bc")
            bcv = cnt["s_bc"]
            W("vector", "s_x", xv)
            W("vector", "s_pd", cnt["s_pd"])
            W("vector", "s_pg", cnt["s_pg"])
            W("vector", "s_puu", cnt["s_puu"])

            if beta == 0:
                W("sync", "s_x", xv)
                DBG("dbg_acc0", acc_t[:, 0, :])
            hv1 = layer_norm(False)
            if beta == 0:
                W("gpsimd", "s_h", hv1)
                DBG("dbg_h1", h_t[:, 0, :], eng="gpsimd", conv=True)
                DBG("dbg_meanb", mean_b[:], eng="gpsimd")
                DBG("dbg_rstdb", rstd_b[:], eng="gpsimd")
                DBG("dbg_thalf", thalf_t[:, 0, :], eng="gpsimd")
                W("sync", "s_stat", cnt["s_stat"])
                DBG("dbg_mean", s_mean[:])
                DBG("dbg_rstd", s_rstd[:])

            ge_base = cnt["s_ge"]
            ev_base = cnt["s_ev"]
            W("tensor", "s_h", hv1)
            W("tensor", "s_bc", bcv)
            W("tensor", "s_ge", ge_base)
            tanh_ge = []
            for nsc in range(8):
                bank = ps_g[nsc % 2]
                if nsc >= 2:
                    W("tensor", "s_ge", tanh_ge[nsc - 2])
                for kc in range(16):
                    if kc < 15:
                        emit("tensor", lambda e, k=kc, n=nsc, b=bank: e.matmul(b[:], bt_t[:, k, 128 * n:128 * n + 128], h_t[:, k, :], start=(k == 0), stop=False))
                    else:
                        emit("tensor", lambda e, n=nsc, b=bank: e.matmul(b[:], bt_t[:, 15, 128 * n:128 * n + 128], h_t[:, 15, :], start=False, stop=True).then_inc(SEM["s_pa"], 1))
                pav = INC("s_pa")
                W("scalar", "s_pa", pav)
                emit("scalar", lambda e, n=nsc, b=bank: e.activation(stb_t[:, n, :], b[:], AF.Tanh).then_inc(SEM["s_ge"], 1))
                tanh_ge.append(INC("s_ge"))
            attn_ev = []
            W("tensor", "s_ev", ev_base)
            for dc in range(16):
                bank = ps_d[dc % 2]
                W("tensor", "s_ge", tanh_ge[7])
                if dc >= 2:
                    W("tensor", "s_ev", attn_ev[dc - 2])
                for nsc in range(8):
                    if nsc < 7:
                        emit("tensor", lambda e, n=nsc, d2=dc, b=bank: e.matmul(b[:], ct_t[:, n, 128 * d2:128 * d2 + 128], stb_t[:, n, :], start=(n == 0), stop=False))
                    else:
                        emit("tensor", lambda e, d2=dc, b=bank: e.matmul(b[:], ct_t[:, 7, 128 * d2:128 * d2 + 128], stb_t[:, 7, :], start=False, stop=True).then_inc(SEM["s_pd"], 1))
                pdv = INC("s_pd")
                W("vector", "s_pd", pdv)
                emit("vector", lambda e, d2=dc, b=bank: e.tensor_tensor(acc_t[:, d2, :], acc_t[:, d2, :], b[:], OP.add))
                emit("vector", lambda e, d2=dc: e.scalar_tensor_tensor(acc_t[:, d2, :], h_t[:, d2, :], dv[:, d2:d2 + 1], acc_t[:, d2, :], OP.mult, OP.add).then_inc(SEM["s_ev"], 1))
                attn_ev.append(INC("s_ev"))

            if beta == 0:
                W("gpsimd", "s_ge", tanh_ge[7])
                DBG("dbg_stb", stb_t[:, 0, :], eng="gpsimd", conv=True)
                W("sync", "s_ev", attn_ev[15])
                DBG("dbg_acc1", acc_t[:, 0, :])

            W("vector", "s_ev", attn_ev[15])
            hv2 = layer_norm(True)

            pa_attn = cnt["s_pa"]
            pd_attn = cnt["s_pd"]
            it_base = cnt["s_it"]
            if beta == 0:
                W("gpsimd", "s_h", hv2)
                DBG("dbg_h2", h_t[:, 0, :], eng="gpsimd", conv=True)
            W("tensor", "s_h", hv2)
            unit_gate = []
            unit_up = []
            grp_pd = []
            gelu_ge = []
            it_h = []
            dn_ev = []
            for j in range(NUNIT):
                nsl = slice(TB * j, TB * (j + 1))
                if j < 2:
                    W("sync", "s_pa", pa_attn)
                    W("sync", "s_pd", pd_attn)
                    W("sync", "s_ev", attn_ev[15])
                else:
                    W("sync", "s_pg", unit_gate[j - 2])
                    W("sync", "s_puu", unit_up[j - 2])
                    gprev = (j - 2) // 2
                    if gprev < len(grp_pd):
                        W("sync", "s_pd", grp_pd[gprev])
                DMA("sync", wg_u(j % 2)[:, :, :], wgS[:, :, nsl].rearrange("k p n -> p k n"), "s_ul")
                DMA("sync", wu_u(j % 2)[:, :, :], wuS[:, :, nsl].rearrange("k p n -> p k n"), "s_ul")
                ulv = DMA("sync", wd_u(j % 2)[:, :, :], wdS[:, 4 * j:4 * j + 4, :], "s_ul")

                W("tensor", "s_ul", ulv)
                for ep in range(4):
                    e_g = 4 * j + ep
                    gb, ub = ps_g[e_g % 2], ps_u[e_g % 2]
                    W("tensor", "s_ge", gelu_ge[e_g - 2] if e_g >= 2 else tanh_ge[7])
                    for kk in range(16):
                        if kk < 15:
                            emit("tensor", lambda e, k=kk, p=ep, b=gb, j2=j: e.matmul(b[:], wg_u(j2 % 2)[:, k, 128 * p:128 * p + 128], h_t[:, k, :], start=(k == 0), stop=False))
                        else:
                            emit("tensor", lambda e, p=ep, b=gb, j2=j: e.matmul(b[:], wg_u(j2 % 2)[:, 15, 128 * p:128 * p + 128], h_t[:, 15, :], start=False, stop=True).then_inc(SEM["s_pg"], 1))
                    pgv = INC("s_pg")
                    W("tensor", "s_it", it_h[e_g - 2] if e_g >= 2 else it_base)
                    for kk in range(16):
                        if kk < 15:
                            emit("tensor", lambda e, k=kk, p=ep, b=ub, j2=j: e.matmul(b[:], wu_u(j2 % 2)[:, k, 128 * p:128 * p + 128], h_t[:, k, :], start=(k == 0), stop=False))
                        else:
                            emit("tensor", lambda e, p=ep, b=ub, j2=j: e.matmul(b[:], wu_u(j2 % 2)[:, 15, 128 * p:128 * p + 128], h_t[:, 15, :], start=False, stop=True).then_inc(SEM["s_puu"], 1))
                    puv = INC("s_puu")
                    W("scalar", "s_pg", pgv)
                    W("scalar", "s_it", it_h[e_g - 2] if e_g >= 2 else it_base)
                    emit("scalar", lambda e, b=gb, g=g1_t(e_g % 2): e.activation(g[:], b[:], AF.Gelu_apprx_tanh).then_inc(SEM["s_ge"], 1))
                    gelu_ge.append(INC("s_ge"))
                    W("vector", "s_ge", gelu_ge[e_g])
                    W("vector", "s_puu", puv)
                    if e_g >= GSZ:
                        W("vector", "s_pd", grp_pd[e_g // GSZ - 1])
                    emit("vector", lambda e, b=ub, g=g1_t(e_g % 2), s2=e_g % GSZ: e.scalar_tensor_tensor(inter_t[:, s2, :], g[:], 0.5, b[:], OP.mult, OP.mult).then_inc(SEM["s_it"], 1))
                    it_h.append(INC("s_it"))
                if beta == 0 and j == 0:
                    W("gpsimd", "s_it", it_h[0])
                    DBG("dbg_inter", inter_t[:, 0, :], eng="gpsimd", conv=True)
                unit_gate.append(cnt["s_pg"])
                unit_up.append(cnt["s_puu"])

                if j % 2 == 1:
                    g = j // 2
                    W("tensor", "s_it", it_h[GSZ * g + GSZ - 1])
                    for hc in range(16):
                        db = ps_d[hc % 2]
                        li = 16 * g + hc
                        if li >= 2:
                            W("tensor", "s_ev", dn_ev[li - 2])
                        else:
                            W("tensor", "s_ev", attn_ev[15])
                        for ei in range(GSZ):
                            e_g = GSZ * g + ei
                            jj = e_g // 4
                            if ei < GSZ - 1:
                                emit("tensor", lambda e, j2=jj, t=e_g % 4, h2=hc, s2=e_g % GSZ, b=db: e.matmul(b[:], wd_u(j2 % 2)[:, t, 128 * h2:128 * h2 + 128], inter_t[:, s2, :], start=(s2 == 0), stop=False))
                            else:
                                emit("tensor", lambda e, j2=jj, t=e_g % 4, h2=hc, s2=e_g % GSZ, b=db: e.matmul(b[:], wd_u(j2 % 2)[:, t, 128 * h2:128 * h2 + 128], inter_t[:, s2, :], start=False, stop=True).then_inc(SEM["s_pd"], 1))
                        pdv = INC("s_pd")
                        W("vector", "s_pd", pdv)
                        emit("vector", lambda e, h2=hc, b=db: e.tensor_tensor(acc_t[:, h2, :], acc_t[:, h2, :], b[:], OP.add).then_inc(SEM["s_ev"], 1))
                        dn_ev.append(INC("s_ev"))
                    grp_pd.append(cnt["s_pd"])

            W("scalar", "s_ev", dn_ev[-1])
            DMA("scalar", oT.rearrange("(c p) t -> p c t", p=128)[:, :, tsl], acc_t[:, :, :], "s_st")
            barrier()

        W("gpsimd", "s_st", cnt["s_st"])

        with nc.Block() as block:
            @block.sync
            def _(eng):
                for fn in progs["sync"]:
                    fn(eng)

            @block.gpsimd
            def _(eng):
                for fn in progs["gpsimd"]:
                    fn(eng)

            @block.vector
            def _(eng):
                for fn in progs["vector"]:
                    fn(eng)

            @block.scalar
            def _(eng):
                for fn in progs["scalar"]:
                    fn(eng)

            @block.tensor
            def _(eng):
                for fn in progs["tensor"]:
                    fn(eng)

    return nc


def kernel(L0_attn_x, L0_attn_B, L0_attn_C, L0_attn_D,
           L0_ffn_gate_W_packed, L0_ffn_gate_scales,
           L0_ffn_up_W_packed, L0_ffn_up_scales,
           L0_ffn_down_W_packed, L0_ffn_down_scales):
    pd_ = _perm(D)
    pn = _perm(NM)

    x = np.ascontiguousarray(np.asarray(L0_attn_x, np.float32).reshape(-1, D).T[pd_])
    BTp = np.ascontiguousarray(np.asarray(L0_attn_B, np.float32).T[pd_])
    CTp = np.ascontiguousarray(np.asarray(L0_attn_C, np.float32).T[:, pd_])
    Dp = np.ascontiguousarray(np.asarray(L0_attn_D, np.float32)[pd_].reshape(16, 128).T)

    def pack_u8(Wp):
        w = np.ascontiguousarray(np.asarray(Wp))
        return w.view(np.uint8).reshape(w.shape[0], w.shape[1], 4)[:, :, 0]

    gpTh = np.ascontiguousarray(pack_u8(L0_ffn_gate_W_packed).T[:, pn])
    upTh = np.ascontiguousarray(pack_u8(L0_ffn_up_W_packed).T[:, pn])
    dpTh = np.ascontiguousarray(pack_u8(L0_ffn_down_W_packed).T[:, pd_])
    gsTh = np.ascontiguousarray(np.asarray(L0_ffn_gate_scales, np.float32).T[:, pn])
    usTh = np.ascontiguousarray(np.asarray(L0_ffn_up_scales, np.float32).T[:, pn])
    dsTh = np.ascontiguousarray(np.asarray(L0_ffn_down_scales, np.float32).T[:, pd_])

    cstv = np.zeros((128, 8), np.float32)
    cstv[:, C_M12] = -12.0
    cstv[:, C_M4] = -4.0
    cstv[:, C_B4] = 4.0
    cstv[:, C_B1275] = 127.5
    cstv[:, C_EPS] = 1e-5
    cstv[:, C_LN05] = LN05
    ones = np.ones((128, 1), np.float32)

    nc = build_program()
    in_maps = []
    for r in range(NCORES):
        in_maps.append(dict(
            xTs=np.ascontiguousarray(x[:, TPC * r:TPC * (r + 1)]),
            gpT=gpTh, upT=upTh, dpT=dpTh, gsT=gsTh, usT=usTh, dsT=dsTh,
            BTp=BTp, CTp=CTp, Dp=Dp, cstI=cstv, onesI=ones,
        ))
    res = run_bass_kernel_spmd(nc, in_maps, core_ids=list(range(NCORES)))
    outT = np.concatenate([r["oT"] for r in res.results], axis=1)
    inv = np.argsort(pd_)
    out = outT[inv].T.reshape(2, 8192, D)
    return np.ascontiguousarray(out, dtype=np.float32)
